# version 03
# baseline (speedup 1.0000x reference)
import math
import contextlib
import numpy as np

import concourse.bass as bass
import concourse.mybir as mybir
from concourse.bass_utils import run_bass_kernel_spmd

F32 = mybir.dt.float32
BF16 = mybir.dt.bfloat16
AF = mybir.ActivationFunctionType
ALU = mybir.AluOpType
AX = mybir.AxisListType

H = 2048
NH = 16
HD = 128
S = 8192
NC = 8
TC = S // NC
FF = 8192
NDC = H // 128
NFC = FF // 128
SCALE = 1.0 / math.sqrt(HD)
EPS5 = 1e-5
EPS6 = 1e-6
CLIP_HI = 128.0 - 1e-6
CLIP_LO = -128.0 + 1e-6
GN = float(FF * H)

ENGINES = ("tensor", "vector", "scalar", "gpsimd", "sync")


class Sem:
    def __init__(self, h, name=""):
        self.h = h
        self.n = 0
        self.name = name

    def at(self):
        return (self, self.n)


class Prog:
    def __init__(self, esems):
        self.q = {e: [] for e in ENGINES}
        self.es = esems

    def op(self, eng, f, waits=()):
        es = self.es[eng]
        wl = [(es, es.n)] + [w for w in waits if w is not None]
        wl = tuple((s.h, v) for (s, v) in wl if v > 0)
        es.n += 1
        self.q[eng].append((f, wl, (es.h, 1)))
        return (es, es.n)

    def raw(self, eng, f, waits=()):
        wl = tuple((s.h, v) for (s, v) in waits if s is not None and v > 0)
        self.q[eng].append((f, wl, None))

    def dma(self, eng, out, in_, sem, waits=()):
        wl = tuple((s.h, v) for (s, v) in
                   (w for w in waits if w is not None) if v > 0)
        sem.n += 16
        self.q[eng].append(
            ((lambda e, o=out, i=in_: e.dma_start(out=o, in_=i)), wl, (sem.h, 16)))
        return (sem, sem.n)

    def emit(self, block):
        def runner(engine, lst):
            for f, waits, inc_t in lst:
                for (h, v) in waits:
                    engine.wait_ge(h, v)
                r = f(engine)
                if inc_t is not None:
                    r.then_inc(*inc_t)

        for name in ENGINES:
            if self.q[name]:
                getattr(block, name)(lambda e, lst=tuple(self.q[name]): runner(e, lst))


class Banks:

    def __init__(self):
        self.wr = [None] * 8
        self.rd = [[] for _ in range(8)]

    def write_waits(self, banks):
        w = []
        for b in banks:
            w += self.rd[b]
        return w

    def wrote(self, banks, comp):
        for b in banks:
            self.wr[b] = comp
            self.rd[b] = []

    def read_waits(self, banks):
        return [self.wr[b] for b in banks if self.wr[b] is not None]

    def read(self, banks, comp):
        for b in banks:
            self.rd[b].append(comp)


def build_nc():
    nc = bass.Bass("TRN2", target_bir_lowering=False)

    x_fm = nc.dram_tensor("x_fm", [H, TC], F32, kind="ExternalInput").ap()
    wqkvT = nc.dram_tensor("wqkvT", [H, 3 * H], F32, kind="ExternalInput").ap()
    woutT = nc.dram_tensor("woutT", [H, H], F32, kind="ExternalInput").ap()
    wfcT = nc.dram_tensor("wfcT", [H, FF], F32, kind="ExternalInput").ap()
    wprojT = nc.dram_tensor("wprojT", [FF, H], F32, kind="ExternalInput").ap()
    bqkv_t = nc.dram_tensor("bqkv_t", [128, 48], F32, kind="ExternalInput").ap()
    bout_t = nc.dram_tensor("bout_t", [128, 16], F32, kind="ExternalInput").ap()
    y_fm = nc.dram_tensor("y_fm", [H, TC], F32, kind="ExternalOutput").ap()

    KVR = 4098
    kv_in = nc.dram_tensor("kv_in", [KVR, 1024], BF16).ap()
    kv_all = nc.dram_tensor("kv_all", [NC * KVR, 1024], BF16).ap()
    gs_in = nc.dram_tensor("gs_in", [2], F32).ap()
    gs_all = nc.dram_tensor("gs_all", [NC * 2], F32).ap()
    xres_pg = nc.dram_tensor("xres_pg", [H, TC], F32).ap()
    gh_pg = nc.dram_tensor("gh_pg", [FF, TC], BF16).ap()
    hq_pg = nc.dram_tensor("hq_pg", [3 * 128, TC], BF16).ap()

    es = contextlib.ExitStack()
    with es:
        def at(name, shape, dtype, off):
            return nc.alloc_sbuf_tensor_at(name, shape, dtype, offset=off).ap()

        identf = at("identf", [128, 128], F32, 18432)
        onesb = at("onesb", [128, 1], BF16, 18944)
        onesrow = at("onesrow", [1, 128], F32, 18976)
        bqkv_s = at("bqkv_s", [128, 48], F32, 19488)
        bout_s = at("bout_s", [128, 16], F32, 19680)
        kscols = at("kscols", [128, 16], F32, 19744)
        thcol = at("thcol", [128, 4], F32, 19808)
        onesrow_bf = at("onesrow_bf", [1, 128], BF16, 19840)
        srow = at("srow", [1, 64], F32, 20096)
        eps5c = at("eps5c", [1, 1], F32, 20352)
        eps6c = at("eps6c", [1, 1], F32, 20384)
        gmrow = at("gmrow", [1, 8], F32, 20416)
        gxcol = at("gxcol", [128, 8], F32, 20448)
        stg8 = at("stg8", [8, 128], F32, 20480)
        gacc = at("gacc", [128, 2], F32, 20992)
        kbcols = at("kbcols", [128, 16], F32, 21024)
        nkbcols = at("nkbcols", [128, 16], F32, 21088)
        qmcols = at("qmcols", [128, 16], F32, 21152)
        nqmcols = at("nqmcols", [128, 16], F32, 21216)
        qmcolsb = at("qmcolsb", [128, 16], BF16, 21280)
        kbcolsb = at("kbcolsb", [128, 16], BF16, 21312)
        rows0 = at("rows0", [1, 1024], F32, 21504)
        rows1 = at("rows1", [1, 1024], F32, 25600)
        m_b = at("m_b", [128, 1024], F32, 29696)
        r_b = at("r_b", [128, 1024], F32, 33792)
        rows2 = r_b[0:1, 0:1024]
        q_b = at("q_b", [128, 1024], F32, 37888)
        s_b = at("s_b", [128, 1024], F32, 41984)
        t1a = at("t1a", [128, 1024], F32, 46080)
        t1b = at("t1b", [128, 1024], F32, 50176)
        sqe = at("sqe", [128, 1024], BF16, 54272)
        sqo = at("sqo", [128, 1024], BF16, 56320)
        hqs2 = at("hqs2", [128, 1024], BF16, 50176)
        xres = at("xres", [128, 16 * 1024], F32, 58368)
        xn = at("xn", [128, 16 * 1024], BF16, 123904)
        kvtK = [at(f"kvtK{i}", [128, 1024], BF16, 123904 + i * 4096) for i in range(2)]
        kvtV = [at(f"kvtV{i}", [128, 1024], BF16, 125952 + i * 4096) for i in range(2)]
        esbt = [at(f"esb{i}", [128, 1024], BF16, 132096 + i * 2048) for i in range(2)]
        ksrow = at("ksrow", [2, 1024], F32, 136192)
        ks_st = [at(f"ks_st{i}", [2, 1024], BF16, 140288 + i * 2048) for i in range(2)]
        dlb = at("dlb", [128, 1024], BF16, 144384)
        nmb = at("nmb", [1, 1024], BF16, 146432)
        fsb = [at(f"fsb{i}", [128, 16 * 128], F32, 132096 + i * 8192) for i in range(2)]
        tsb = [at(f"tsb{i}", [128, 16 * 128], BF16, 148480 + i * 4096) for i in range(2)]
        qfb = at("qfb", [128, 16 * 1024], BF16, 156672)
        ao = at("ao", [128, 16 * 1024], BF16, 189440)
        gsc = [at(f"gsc{i}", [128, 2048], F32, 189440 + i * 8192) for i in range(2)]
        wsb = [at("wsb0", [128, 16 * 256], BF16, 205824),
               at("wsb1", [128, 16 * 256], BF16, 214016)]
        wob = [at(f"wob{i}", [128, 16 * 128], BF16, 189440 + i * 4096) for i in range(2)]
        hqx0 = at("hqx0", [128, 1024], BF16, 189440)
        hqx1 = at("hqx1", [128, 1024], BF16, 191488)
        hq_off = ([58368 + i * 2048 for i in range(32)] +
                  [123904 + i * 2048 for i in range(4)] +
                  [156672 + i * 2048 for i in range(16)] +
                  [222208, 224256, 226304] +
                  [21504, 23552, 25600, 27648] +
                  [33792, 35840])
        hqt = [at(f"hq{i}", [128, 1024], BF16, hq_off[i]) for i in range(61)]
        pA = es.enter_context(nc.psum_tensor("pA", [128, 1024], F32)).ap()
        pB = es.enter_context(nc.psum_tensor("pB", [128, 1024], F32)).ap()
        pC = es.enter_context(nc.psum_tensor("pC", [128, 1024], F32)).ap()
        pD = es.enter_context(nc.psum_tensor("pD", [128, 1024], F32)).ap()
        PB = {"pA": (pA, (0, 1)), "pB": (pB, (2, 3)), "pC": (pC, (4, 5)), "pD": (pD, (6, 7))}

        esems = {e: Sem(es.enter_context(nc.semaphore("e_" + e)), e) for e in ENGINES}
        dn = ["s_x", "s_w0", "s_w1", "s_f0", "s_f1", "s_k0", "s_k1", "s_kvw",
              "s_cc", "s_nr", "s_go0", "s_go1", "s_gi0", "s_gi1", "s_pg",
              "s_pgi", "s_out", "s_g0", "s_g1", "s_fl", "s_b0", "s_sq", "s_o1"]
        ds = {n: Sem(es.enter_context(nc.semaphore(n)), n) for n in dn}
        block = es.enter_context(nc.Block())
        P = Prog(esems)
        BK = Banks()

        def mm_group(pname, half, lhsT_rhs, waits_first=(), start=True, stop=True):
            p, bk = PB[pname]
            bank = [bk[half]]
            w = list(waits_first) + (BK.write_waits(bank) if start else [])
            comp = None
            n = len(lhsT_rhs)
            for i, (l, r) in enumerate(lhsT_rhs):
                st = start and (i == 0)
                sp = stop and (i == n - 1)
                comp = P.op("tensor",
                            (lambda t, l=l, r=r, st=st, sp=sp, p=p, half=half:
                             t.matmul(p[0:128, half * 512:(half + 1) * 512], l, r,
                                      start=st, stop=sp, skip_group_check=True)),
                            waits=(w if i == 0 else ()))
                w = []
            if stop:
                BK.wrote(bank, comp)
            return comp

        def mm_row_group(pname, half, rhs_list, waits_first=(), start=True, stop=True):
            p, bk = PB[pname]
            bank = [bk[half]]
            w = list(waits_first) + (BK.write_waits(bank) if start else [])
            comp = None
            n = len(rhs_list)
            for i, r in enumerate(rhs_list):
                st = start and (i == 0)
                sp = stop and (i == n - 1)
                comp = P.op("tensor",
                            (lambda t, r=r, st=st, sp=sp, p=p, half=half:
                             t.matmul(p[0:1, half * 512:(half + 1) * 512], onesb[:], r,
                                      start=st, stop=sp, skip_group_check=True)),
                            waits=(w if i == 0 else ()))
                w = []
            if stop:
                BK.wrote(bank, comp)
            return comp

        def psum_read(eng, pname, f, waits=(), halves=(0, 1)):
            p, bk = PB[pname]
            banks = [bk[h] for h in halves]
            comp = P.op(eng, f, waits=list(waits) + BK.read_waits(banks))
            BK.read(banks, comp)
            return comp

        def bcast_row(row_ap, dest, waits=(), pname="pB"):
            p, bk = PB[pname]
            for hf in range(2):
                w = (list(waits) if hf == 0 else []) + BK.write_waits([bk[hf]])
                comp = P.op("tensor",
                            (lambda t, hf=hf, p=p: t.matmul(
                                p[0:128, hf * 512:(hf + 1) * 512], onesrow[:],
                                row_ap[0:1, hf * 512:(hf + 1) * 512],
                                start=True, stop=True, skip_group_check=True)),
                            waits=w)
                BK.wrote([bk[hf]], comp)
            comp = psum_read("vector", pname,
                             (lambda v, p=p: v.tensor_copy(dest[0:128, 0:1024], p[0:128, 0:1024])))
            return comp

        def xslice(buf, c, w=1024):
            return buf[0:128, c * w:(c + 1) * w]

        def ln_stats(tiles_bf, eps_ap, dep=(), nchunk=16, scale=None):
            n = nchunk
            inv = 1.0 / (n * 128.0)
            for hf in range(2):
                rhss = [tiles_bf(c)[0:128, hf * 512:(hf + 1) * 512] for c in range(n)]
                mm_row_group("pC", hf, rhss, waits_first=list(dep))
            prev_mm = None
            for c in range(n):
                sq = P.op("scalar",
                          (lambda s, c=c: s.activation(sqo[:], tiles_bf(c), AF.Square)),
                          waits=list(dep) + [prev_mm])
                for hf in range(2):
                    prev_mm = mm_row_group("pD", hf, [sqo[0:128, hf * 512:(hf + 1) * 512]],
                                           waits_first=[sq], start=(c == 0), stop=(c == n - 1))
            m_c = psum_read("vector", "pC",
                            (lambda v: v.tensor_scalar_mul(rows0[:], pC[0:1, 0:1024], inv)))
            e_c = psum_read("vector", "pD",
                            (lambda v: v.tensor_scalar_mul(rows1[:], pD[0:1, 0:1024], inv)))
            P.op("vector", lambda v: v.tensor_tensor(rows2[:], rows0[:], rows0[:], ALU.mult),
                 waits=[m_c])
            P.op("vector", lambda v: v.tensor_tensor(rows1[:], rows1[:], rows2[:], ALU.subtract),
                 waits=[e_c])
            vcomp = esems["vector"].at()
            sd = P.op("scalar",
                      (lambda s: s.activation(rows1[:], rows1[:], AF.Sqrt,
                                              bias=eps_ap, scale=1.0)),
                      waits=[vcomp])
            r_c = P.op("vector", lambda v: v.reciprocal(rows1[:], rows1[:]), waits=[sd])
            return m_c, r_c

        c_x = None
        for c in range(16):
            c_x = P.dma("gpsimd", xslice(xres, c), x_fm[c * 128:(c + 1) * 128, 0:TC],
                        ds["s_x"])
        P.dma("sync", bqkv_s[:], bqkv_t[:], ds["s_b0"])
        c_bq = c_bo = P.dma("sync", bout_s[:], bout_t[:], ds["s_b0"])

        c_i0 = P.op("gpsimd", lambda g: g.iota(identf[:], [[1, 128]], channel_multiplier=-1,
                                               allow_small_or_imprecise_dtypes=True))
        c_id = P.op("vector", lambda v: v.tensor_single_scalar(identf[:], identf[:], 0.0,
                                                               ALU.is_equal), waits=[c_i0])
        P.op("vector", lambda v: v.memset(onesb[:], 1.0))
        P.op("vector", lambda v: v.memset(onesrow[:], 1.0))
        P.op("vector", lambda v: v.memset(onesrow_bf[:], 1.0))
        P.op("vector", lambda v: v.memset(eps5c[:], EPS5))
        c_eps = P.op("vector", lambda v: v.memset(eps6c[:], EPS6))
        c_consts = esems["vector"].at()

        pass

        gfc_sl = nc.dram_tensor("gfc_sl", [256, FF], F32, kind="ExternalInput").ap()
        gpj_sl = nc.dram_tensor("gpj_sl", [1024, H], F32, kind="ExternalInput").ap()

        gtiles = []
        for t in range(8):
            src = gfc_sl[(t % 2) * 128:(t % 2) * 128 + 128, (t // 2) * 2048:(t // 2) * 2048 + 2048]
            gtiles.append(src)
        for t in range(8):
            src = gpj_sl[t * 128:(t + 1) * 128, 0:2048]
            gtiles.append(src)
        gcomps = []
        prev_use = [None, None]
        for t in range(16):
            slot = t % 2
            c_l = P.dma("sync", gsc[slot][:], gtiles[t], ds["s_g%d" % slot],
                        waits=[prev_use[slot]])
            which = 0 if t < 8 else 1
            if t in (0, 8):
                c_r = P.op("vector",
                           (lambda v, slot=slot, which=which: v.tensor_reduce(
                               gacc[0:128, which:which + 1], gsc[slot][:], AX.X, ALU.add,
                               apply_absolute_value=True)),
                           waits=[c_l])
            else:
                c_r0 = P.op("vector",
                            (lambda v, slot=slot: v.tensor_reduce(
                                gxcol[0:128, 0:1], gsc[slot][:], AX.X, ALU.add,
                                apply_absolute_value=True)),
                            waits=[c_l])
                c_r = P.op("vector",
                           (lambda v, which=which: v.tensor_tensor(
                               gacc[0:128, which:which + 1], gacc[0:128, which:which + 1],
                               gxcol[0:128, 0:1], ALU.add)),
                           waits=[c_r0])
            prev_use[slot] = c_r
            gcomps.append(c_r)
        c_gt = P.op("tensor", lambda t: t.transpose(pA[0:2, 0:128], gacc[:], identf[:]),
                    waits=[gcomps[-1], gcomps[7], c_id] + BK.write_waits([0]))
        BK.wrote([0], c_gt)
        c_gr = psum_read("vector", "pA",
                         (lambda v: v.tensor_reduce(gacc[0:2, 0:1], pA[0:2, 0:128],
                                                    AX.X, ALU.add)), halves=(0,))
        c_gw = P.dma("gpsimd", gs_in[0:2], gacc[0:2, 0:1], ds["s_fl"], waits=[c_gr])
        c_gg = P.op("gpsimd",
                    (lambda g: g.collective_compute(
                        "AllGather", ALU.bypass, replica_groups=[list(range(NC))],
                        ins=[gs_in.opt()], outs=[gs_all.opt()])),
                    waits=[c_gw])
        c_ga = P.dma("gpsimd", srow[0:1, 32:48], gs_all[0:16], ds["s_fl"], waits=[c_gg])
        c_s = P.op("vector", lambda v: v.tensor_copy(gmrow[0:1, 0:2], srow[0:1, 32:34]),
                   waits=[c_ga])
        for r in range(1, 8):
            c_s = P.op("vector",
                       (lambda v, r=r: v.tensor_tensor(gmrow[0:1, 0:2], gmrow[0:1, 0:2],
                                                       srow[0:1, 32 + 2 * r:34 + 2 * r], ALU.add)),
                       waits=[c_s])
        c_s = P.op("vector", lambda v: v.tensor_scalar_mul(gmrow[0:1, 0:2], gmrow[0:1, 0:2],
                                                           1.0 / GN), waits=[c_s])
        c_t1 = P.op("vector", lambda v: v.tensor_scalar(gmrow[0:1, 2:3], gmrow[0:1, 0:1],
                                                        0.5, 0.5 * EPS6, ALU.mult, ALU.add),
                    waits=[c_s])
        c_t2 = P.op("vector", lambda v: v.tensor_scalar(gmrow[0:1, 4:5], gmrow[0:1, 1:2],
                                                        0.5, 0.5 * EPS6, ALU.mult, ALU.add),
                    waits=[c_s])
        c_t3 = P.op("vector", lambda v: v.tensor_scalar_mul(gmrow[0:1, 3:4], gmrow[0:1, 2:3],
                                                            -1.0), waits=[c_t1])
        c_t4 = P.op("vector", lambda v: v.tensor_scalar_mul(gmrow[0:1, 5:6], gmrow[0:1, 4:5],
                                                            -1.0), waits=[c_t2])
        p, bk = PB["pB"]
        c_th0 = P.op("tensor", lambda t: t.matmul(pB[0:128, 0:4], onesrow[:], gmrow[0:1, 2:6],
                                                  start=True, stop=True, skip_group_check=True),
                     waits=[c_t3, c_t4, c_consts] + BK.write_waits([2]))
        BK.wrote([2], c_th0)
        c_th = psum_read("vector", "pB",
                         (lambda v: v.tensor_copy(thcol[:], pB[0:128, 0:4])), halves=(0,))

        c_x16 = ds["s_x"].at()
        casts = []
        for c in range(16):
            casts.append(P.op("scalar",
                              (lambda s, c=c: s.activation(xslice(xn, c), xslice(xres, c),
                                                           AF.Copy)),
                              waits=[c_x16] if c == 0 else []))
        m_c, r_c = ln_stats(lambda c: xslice(xn, c), eps5c[0:1, 0:1],
                            dep=[casts[-1], c_consts])
        cmb = bcast_row(rows0, m_b, waits=[m_c], pname="pB")
        crb = bcast_row(rows1, r_b, waits=[r_c], pname="pB")
        apply_done = []
        for c in range(16):
            u = P.op("vector", (lambda v, c=c: v.tensor_tensor(t1a[:], xslice(xres, c),
                                                               m_b[:], ALU.subtract)),
                     waits=[cmb, crb] if c == 0 else [])
            d = P.op("vector", (lambda v, c=c: v.tensor_tensor(xslice(xn, c), t1a[:],
                                                               r_b[:], ALU.mult)))
            apply_done.append(d)
        c_xn1 = apply_done[-1]

        WQD = ds["s_w0"], ds["s_w1"]
        slab_free = [None, None]
        slab_loaded = [None, None]
        qn_done = []
        kn_done = []
        kv_wr = []
        sqe_users = []
        sqo_free = None
        t1a_free = None

        def load_slab(slot, dram, col0, width, sem, prev):
            comp = None
            for c in range(16):
                comp = P.dma("gpsimd", wsb[slot][0:128, c * width:(c + 1) * width],
                             dram[c * 128:(c + 1) * 128, col0:col0 + width],
                             sem, waits=[prev] if c == 0 else [])
            return comp

        for g in range(24):
            slot = g % 2
            slab_loaded[slot] = load_slab(slot, wqkvT, g * 256, 256, WQD[slot],
                                          slab_free[slot])
            for j in range(2):
                oc = 2 * g + j
                pname = "pA" if (oc % 2 == 0) else "pB"
                pp, pbk = PB[pname]
                for hf in range(2):
                    pairs = [(wsb[slot][0:128, c * 256 + j * 128: c * 256 + j * 128 + 128],
                              xn[0:128, c * 1024 + hf * 512: c * 1024 + hf * 512 + 512])
                             for c in range(16)]
                    mmc = mm_group(pname, hf, pairs,
                                   waits_first=[slab_loaded[slot], c_xn1])
                slab_free[slot] = mmc
                if oc < 16:
                    h = oc
                    e = psum_read("scalar", pname,
                                  (lambda s, h=h, pp=pp: s.activation(
                                      xslice(qfb, h), pp[0:128, 0:1024], AF.Identity,
                                      bias=bqkv_s[0:128, h:h + 1], scale=1.0)),
                                  waits=[c_bq])
                elif oc < 32:
                    h = oc - 16
                    e = psum_read("scalar", pname,
                                  (lambda s, h=h, pp=pp: s.activation(
                                      sqe[:], pp[0:128, 0:1024], AF.Identity,
                                      bias=bqkv_s[0:128, 16 + h:17 + h], scale=1.0)),
                                  waits=[c_bq] + sqe_users)
                    kdma = P.dma("gpsimd", kv_in[h * 128:(h + 1) * 128, 0:1024], sqe[:],
                                 ds["s_sq"], waits=[e])
                    kv_wr.append(kdma)
                    ksr = P.op("vector", (lambda v, h=h: v.tensor_reduce(
                        kscols[0:128, h:h + 1], sqe[:], AX.X, ALU.add)), waits=[e])
                    sqe_users = [kdma, ksr]
                    kn_done.append(ksr)
                else:
                    h = oc - 32
                    e = psum_read("vector", pname,
                                  (lambda v, h=h, pp=pp: v.tensor_scalar(
                                      t1a[:], pp[0:128, 0:1024],
                                      bqkv_s[0:128, 32 + h:33 + h], None, ALU.add)),
                                  waits=[c_bq, t1a_free])
                    vds = []
                    for b in range(8):
                        bk6 = 6 + (b % 2)
                        cw = P.op("tensor",
                                  (lambda t, b=b: t.transpose(
                                      pD[0:128, (b % 2) * 512:(b % 2) * 512 + 128],
                                      t1a[0:128, b * 128:(b + 1) * 128], identf[:])),
                                  waits=[e, c_id] + BK.write_waits([bk6]))
                        BK.wrote([bk6], cw)
                        cv = psum_read("vector", "pD",
                                       (lambda v, b=b: v.tensor_copy(
                                           sqe[0:128, b * 128:(b + 1) * 128],
                                           pD[0:128, (b % 2) * 512:(b % 2) * 512 + 128])),
                                       waits=sqe_users if b == 0 else [],
                                       halves=((b % 2),))
                        vd = P.dma("gpsimd",
                                   kv_in[2048 + h * 128 + b * 16:2048 + h * 128 + (b + 1) * 16,
                                         0:1024],
                                   sqe[0:128, b * 128:(b + 1) * 128],
                                   ds["s_sq"], waits=[cv])
                        kv_wr.append(vd)
                        vds.append(vd)
                    sqe_users = [vds[-1]]
                    t1a_free = cw

        c_kt = P.op("tensor", lambda t: t.transpose(pC[0:16, 0:128], kscols[:], identf[:]),
                    waits=[kn_done[-1], c_id] + BK.write_waits([4]))
        BK.wrote([4], c_kt)
        c_ke = psum_read("vector", "pC",
                         (lambda v: v.tensor_copy(sqe[0:16, 0:128], pC[0:16, 0:128])),
                         waits=sqe_users, halves=(0,))
        kv_wr.append(P.dma("gpsimd", kv_in[4096:4098, 0:1024], sqe[0:16, 0:128],
                           ds["s_sq"], waits=[c_ke]))

        c_kv = P.op("gpsimd",
                    (lambda g: g.collective_compute(
                        "AllGather", ALU.bypass, replica_groups=[list(range(NC))],
                        ins=[kv_in.opt()], outs=[kv_all.opt()])),
                    waits=[kv_wr[-1], ds["s_sq"].at()])

        c_ks = [None, None]
        NR = (ds["s_nr"], ds["s_kvw"])
        acc_c = None
        for r in range(NC):
            sl = r % 2
            c_g = P.dma("gpsimd", ks_st[sl][:],
                        kv_all[r * KVR + 4096:r * KVR + 4098, 0:1024], NR[sl],
                        waits=[c_ks[sl], c_kv if r < 2 else None])
            if r == 0:
                acc_c = P.op("vector", lambda v: v.tensor_copy(ksrow[:], ks_st[0][:]),
                             waits=[c_g])
            else:
                acc_c = P.op("vector", (lambda v, sl=sl: v.tensor_tensor(
                    ksrow[:], ksrow[:], ks_st[sl][:], ALU.add)), waits=[c_g])
            c_ks[sl] = acc_c
        c_km = P.op("vector", lambda v: v.tensor_scalar_mul(ksrow[:], ksrow[:], 1.0 / S),
                    waits=[acc_c])
        c_kb = None
        for h in range(NH):
            j = h // 8
            c_kb = P.dma("gpsimd", kbcols[0:128, h:h + 1],
                         ksrow[j:j + 1, (h % 8) * 128:(h % 8) * 128 + 128],
                         ds["s_fl"], waits=[c_km] if h == 0 else [])
        c_kb = P.op("vector", lambda v: v.tensor_copy(kbcolsb[:], kbcols[:]), waits=[c_kb])
        c_nkb = P.op("vector", lambda v: v.tensor_scalar_mul(nkbcols[:], kbcols[:], -1.0))
        c_sall = esems["scalar"].at()
        c_qm = None
        for h in range(NH):
            c_qm = P.op("vector", (lambda v, h=h: v.tensor_reduce(
                qmcols[0:128, h:h + 1], xslice(qfb, h), AX.X, ALU.add)),
                waits=[c_sall] if h == 0 else [])
        c_qm = P.op("vector", lambda v: v.tensor_scalar_mul(qmcols[:], qmcols[:], 1.0 / TC),
                    waits=[c_qm])
        c_nqm = P.op("vector", lambda v: v.tensor_scalar_mul(nqmcols[:], qmcols[:], -1.0))
        c_qmb = P.op("vector", lambda v: v.tensor_copy(qmcolsb[:], qmcols[:]))
        KS = ds["s_k0"], ds["s_k1"]
        kv_free = [None, None]
        dl_free = None
        for h in range(NH):
            for r in range(NC):
                slot = (h * NC + r) % 2
                ck = P.dma("gpsimd", kvtK[slot][:],
                           kv_all[r * KVR + h * 128:r * KVR + (h + 1) * 128, 0:1024],
                           KS[slot],
                           waits=[kv_free[slot], c_kv if kv_free[slot] is None else None])
                cd = P.op("scalar", (lambda s, slot=slot, h=h: s.activation(
                    dlb[:], kvtK[slot][:], AF.Identity,
                    bias=nkbcols[0:128, h:h + 1], scale=1.0)),
                    waits=[ck, c_nkb, dl_free])
                kv_free[slot] = cd
                csq = P.op("scalar", lambda s: s.activation(sqo[:], dlb[:], AF.Square),
                           waits=[sqo_free])
                smm = None
                for hf in range(2):
                    smm = mm_row_group("pC", hf, [sqo[0:128, hf * 512:(hf + 1) * 512]],
                                       waits_first=[csq])
                for hf in range(2):
                    w = BK.write_waits([6 + hf])
                    qmm = P.op("tensor",
                               (lambda t, hf=hf, h=h: t.matmul(
                                   pD[0:1, hf * 512:(hf + 1) * 512],
                                   qmcolsb[0:128, h:h + 1],
                                   dlb[0:128, hf * 512:(hf + 1) * 512],
                                   start=True, stop=True, skip_group_check=True)),
                               waits=w + ([cd, c_qmb] if hf == 0 else []))
                    BK.wrote([6 + hf], qmm)
                sqo_free = smm
                dl_free = qmm
                if r == 0:
                    m1 = psum_read("vector", "pC",
                                   (lambda v, h=h: v.tensor_reduce(
                                       srow[0:1, h:h + 1], pC[0:1, 0:1024], AX.X, ALU.max)))
                    m2 = psum_read("vector", "pD",
                                   (lambda v, h=h: v.tensor_reduce(
                                       srow[0:1, 16 + h:17 + h], pD[0:1, 0:1024],
                                       AX.X, ALU.max)))
                else:
                    m1 = psum_read("vector", "pC",
                                   (lambda v: v.tensor_reduce(
                                       gmrow[0:1, 6:7], pC[0:1, 0:1024], AX.X, ALU.max)))
                    m1 = P.op("vector", (lambda v, h=h: v.tensor_tensor(
                        srow[0:1, h:h + 1], srow[0:1, h:h + 1], gmrow[0:1, 6:7], ALU.max)))
                    m2 = psum_read("vector", "pD",
                                   (lambda v: v.tensor_reduce(
                                       gmrow[0:1, 7:8], pD[0:1, 0:1024], AX.X, ALU.max)))
                    m2 = P.op("vector", (lambda v, h=h: v.tensor_tensor(
                        srow[0:1, 16 + h:17 + h], srow[0:1, 16 + h:17 + h],
                        gmrow[0:1, 7:8], ALU.max)))
        c_dk = P.op("scalar", lambda s: s.activation(srow[0:1, 0:16], srow[0:1, 0:16],
                                                     AF.Sqrt), waits=[m1, m2])
        c_qk = P.op("vector", lambda v: v.tensor_scalar_add(srow[0:1, 16:32],
                                                            srow[0:1, 16:32], 24.0),
                    waits=[c_dk])

        esb_free = [None, None]
        ao_done = []
        for h in range(NH):
            for hf in range(2):
                w = BK.write_waits([6 + hf])
                cqk = P.op("tensor",
                           (lambda t, hf=hf, h=h: t.matmul(
                               pD[0:1, hf * 512:(hf + 1) * 512],
                               kbcolsb[0:128, h:h + 1],
                               qfb[0:128, h * 1024 + hf * 512:h * 1024 + hf * 512 + 512],
                               start=True, stop=True, skip_group_check=True)),
                           waits=w + ([c_kb, c_qk] if hf == 0 else []))
                BK.wrote([6 + hf], cqk)
            cdq = P.op("scalar", (lambda s, h=h: s.activation(
                dlb[:], xslice(qfb, h), AF.Identity,
                bias=nqmcols[0:128, h:h + 1], scale=1.0)), waits=[dl_free, c_nqm])
            csq = P.op("scalar", lambda s: s.activation(sqo[:], dlb[:], AF.Square),
                       waits=[sqo_free, cdq])
            dl_free = csq
            smm = None
            for hf in range(2):
                smm = mm_row_group("pC", hf, [sqo[0:128, hf * 512:(hf + 1) * 512]],
                                   waits_first=[csq])
            sqo_free = smm
            cdqr = psum_read("scalar", "pC",
                             (lambda s: s.activation(rows0[:], pC[0:1, 0:1024], AF.Sqrt)))
            cnm = psum_read("vector", "pD",
                            (lambda v, h=h: v.scalar_tensor_tensor(
                                rows1[:], rows0[:], srow[0:1, h:h + 1],
                                pD[0:1, 0:1024], ALU.mult, ALU.add)),
                            waits=[cdqr])
            cnm = P.op("vector", (lambda v, h=h: v.tensor_scalar(
                rows1[:], rows1[:], srow[0:1, 16 + h:17 + h], -1.0,
                ALU.add, ALU.mult)), waits=[cnm])
            cnm = P.op("vector", lambda v: v.tensor_copy(nmb[0:1, 0:1024], rows1[:]),
                       waits=[cnm])
            av_pairs_first = True
            dn_first = True
            av_comp = None
            dn_comp = None
            for r in range(NC):
                slot = (h * NC + r) % 2
                base = r * KVR
                ck = P.dma("gpsimd", kvtK[slot][:],
                           kv_all[base + h * 128:base + (h + 1) * 128, 0:1024],
                           KS[slot],
                           waits=[kv_free[slot],
                                  c_kv if kv_free[slot] is None else None])
                cv = None
                for b in range(8):
                    cv = P.dma("gpsimd", kvtV[slot][0:128, b * 128:(b + 1) * 128],
                               kv_all[base + 2048 + h * 128 + b * 16:
                                      base + 2048 + h * 128 + (b + 1) * 16, 0:1024],
                               KS[slot])
                kv_loaded = cv
                for s8 in range(8):
                    sp = "pA" if ((r * 8 + s8) % 2 == 0) else "pB"
                    pp, pbk = PB[sp]
                    eslot = (r * 8 + s8) % 2
                    for hf in range(2):
                        mm_group(sp, hf,
                                 [(kvtK[slot][0:128, s8 * 128:(s8 + 1) * 128],
                                   qfb[0:128, h * 1024 + hf * 512:h * 1024 + hf * 512 + 512]),
                                  (onesrow_bf[0:1, 0:128],
                                   nmb[0:1, hf * 512:(hf + 1) * 512])],
                                 waits_first=[kv_loaded, cnm])
                    ex = psum_read("scalar", sp,
                                   (lambda s, eslot=eslot, pp=pp: s.activation(
                                       esbt[eslot][:], pp[0:128, 0:1024], AF.Exp,
                                       scale=SCALE)),
                                   waits=[esb_free[eslot]])
                    for hf in range(2):
                        av_comp = mm_group("pC", hf,
                                           [(kvtV[slot][0:128, s8 * 128:(s8 + 1) * 128],
                                             esbt[eslot][0:128, hf * 512:hf * 512 + 512])],
                                           waits_first=[ex],
                                           start=av_pairs_first, stop=(r == 7 and s8 == 7))
                        dn_comp = mm_row_group("pD", hf,
                                               [esbt[eslot][0:128, hf * 512:hf * 512 + 512]],
                                               waits_first=[ex],
                                               start=dn_first, stop=(r == 7 and s8 == 7))
                    av_pairs_first = False
                    dn_first = False
                    esb_free[eslot] = av_comp
                kv_free[slot] = av_comp
            c_d = psum_read("vector", "pD",
                            (lambda v: v.reciprocal(rows2[:], pD[0:1, 0:1024])),
                            waits=[dn_comp])
            c_rb = bcast_row(rows2, m_b, waits=[c_d], pname="pA")
            c_ao = psum_read("vector", "pC",
                             (lambda v, h=h: v.tensor_tensor(xslice(ao, h),
                                                             pC[0:128, 0:1024], m_b[:],
                                                             ALU.mult)),
                             waits=[c_rb, av_comp])
            ao_done.append(c_ao)

        m_c, r_c = ln_stats(lambda c: xslice(ao, c), eps5c[0:1, 0:1], dep=[ao_done[-1]])
        cmb = bcast_row(rows0, m_b, waits=[m_c], pname="pB")
        crb = bcast_row(rows1, r_b, waits=[r_c], pname="pB")
        aon_done = []
        for c in range(16):
            P.op("vector", (lambda v, c=c: v.tensor_tensor(t1a[:], xslice(ao, c), m_b[:],
                                                           ALU.subtract)),
                 waits=[cmb, crb] if c == 0 else [])
            d = P.op("vector", (lambda v, c=c: v.tensor_tensor(xslice(xn, c), t1a[:],
                                                               r_b[:], ALU.mult)))
            aon_done.append(d)
        c_aon = aon_done[-1]

        wob_free = [esems["tensor"].at(), esems["tensor"].at()]
        z_done = []
        PN4 = ["pA", "pB", "pC", "pD"]
        deferred = []
        for i, oc in enumerate(range(15, -1, -1)):
            slot = i % 2
            loaded = None
            for c in range(16):
                loaded = P.dma("gpsimd", wob[slot][0:128, c * 128:(c + 1) * 128],
                               woutT[c * 128:(c + 1) * 128, oc * 128:(oc + 1) * 128],
                               WQD[slot],
                               waits=[wob_free[slot], c_aon if i < 2 else None]
                               if c == 0 else [])
            pname = PN4[oc] if oc < 4 else ("pA" if (i % 2 == 0) else "pB")
            pp, pbk = PB[pname]
            for hf in range(2):
                pairs = [(wob[slot][0:128, c * 128:(c + 1) * 128],
                          xn[0:128, c * 1024 + hf * 512: c * 1024 + hf * 512 + 512])
                         for c in range(16)]
                mmc = mm_group(pname, hf, pairs, waits_first=[loaded, c_aon])
            wob_free[slot] = mmc
            if oc < 4:
                deferred.append((oc, pname, pp, mmc))
            else:
                e = psum_read("scalar", pname,
                              (lambda s, oc=oc, pp=pp: s.activation(
                                  xslice(ao, oc), pp[0:128, 0:1024], AF.Identity,
                                  bias=bout_s[0:128, oc:oc + 1], scale=1.0)),
                              waits=[c_bo])
                z_done.append(e)
        for (oc, pname, pp, mmc) in deferred:
            e = psum_read("scalar", pname,
                          (lambda s, oc=oc, pp=pp: s.activation(
                              xslice(ao, oc), pp[0:128, 0:1024], AF.Identity,
                              bias=bout_s[0:128, oc:oc + 1], scale=1.0)),
                          waits=[c_bo, wob_free[0], wob_free[1]])
            z_done.append(e)

        m_c, r_c = ln_stats(lambda c: xslice(ao, c), eps5c[0:1, 0:1], dep=[z_done[-1]])
        cmb = bcast_row(rows0, m_b, waits=[m_c], pname="pB")
        crb = bcast_row(rows1, r_b, waits=[r_c], pname="pB")
        res_done = []
        for c in range(16):
            P.op("vector", (lambda v, c=c: v.tensor_tensor(t1a[:], xslice(ao, c), m_b[:],
                                                           ALU.subtract)),
                 waits=[cmb, crb] if c == 0 else [])
            P.op("vector", lambda v: v.tensor_tensor(t1b[:], t1a[:], r_b[:], ALU.mult))
            d = P.op("vector", (lambda v, c=c: v.tensor_tensor(xslice(xres, c),
                                                               xslice(xres, c), t1b[:],
                                                               ALU.add)))
            res_done.append(d)
        c_x2 = res_done[-1]

        casts = []
        for c in range(16):
            casts.append(P.op("scalar",
                              (lambda s, c=c: s.activation(xslice(xn, c), xslice(xres, c),
                                                           AF.Copy)),
                              waits=[c_x2] if c == 0 else []))
        m_c, r_c = ln_stats(lambda c: xslice(xn, c), eps5c[0:1, 0:1], dep=[casts[-1]])
        cmb = bcast_row(rows0, m_b, waits=[m_c], pname="pB")
        crb = bcast_row(rows1, r_b, waits=[r_c], pname="pB")
        xn2_done = []
        for c in range(16):
            P.op("vector", (lambda v, c=c: v.tensor_tensor(t1a[:], xslice(xres, c), m_b[:],
                                                           ALU.subtract)),
                 waits=[cmb, crb] if c == 0 else [])
            d = P.op("vector", (lambda v, c=c: v.tensor_tensor(xslice(xn, c), t1a[:],
                                                               r_b[:], ALU.mult)))
            xn2_done.append(d)
        c_xn2 = xn2_done[-1]
        c_pg = None
        for c in range(16):
            c_pg = P.dma("gpsimd", xres_pg[c * 128:(c + 1) * 128, 0:TC], xslice(xres, c),
                         ds["s_pg"], waits=[c_xn2] if c == 0 else [])
        c_pg_all = ds["s_pg"].at()

        m_c, r_c = ln_stats(lambda c: xslice(xn, c), eps6c[0:1, 0:1], dep=[c_xn2])
        cmb = bcast_row(rows0, m_b, waits=[m_c], pname="pB")
        macc_c = None
        for c in range(16):
            u = P.op("vector", (lambda v, c=c: v.tensor_tensor(t1a[:], xslice(xn, c),
                                                               m_b[:], ALU.subtract)),
                     waits=[cmb] if c == 0 else [])
            a = P.op("scalar", lambda s: s.activation(sqo[:], t1a[:], AF.Abs),
                     waits=[u, macc_c, sqo_free])
            if c == 0:
                macc_c = P.op("vector", lambda v: v.tensor_copy(q_b[:], sqo[:]), waits=[a])
            else:
                macc_c = P.op("vector", lambda v: v.tensor_tensor(q_b[:], q_b[:], sqo[:],
                                                                  ALU.max), waits=[a])

        def gx_dance(macc_tile, dep):
            comps = []
            for b in range(8):
                cw = P.op("tensor",
                          (lambda t, b=b: t.transpose(
                              pA[0:128, (b % 2) * 512:(b % 2) * 512 + 128],
                              macc_tile[0:128, b * 128:(b + 1) * 128], identf[:])),
                          waits=[dep, c_id] + BK.write_waits([0 + (b % 2)]))
                BK.wrote([0 + (b % 2)], cw)
                cr = psum_read("vector", "pA",
                               (lambda v, b=b: v.tensor_reduce(
                                   gxcol[0:128, b:b + 1],
                                   pA[0:128, (b % 2) * 512:(b % 2) * 512 + 128],
                                   AX.X, ALU.max)),
                               halves=((b % 2),))
                comps.append(cr)
            ct = P.op("tensor", lambda t: t.transpose(pA[0:8, 0:128], gxcol[:], identf[:]),
                      waits=[comps[-1]] + BK.write_waits([0]))
            BK.wrote([0], ct)
            cs = psum_read("vector", "pA",
                           (lambda v: v.tensor_copy(stg8[:], pA[0:8, 0:128])), halves=(0,))
            cf = P.dma("gpsimd", rows2[0:1, 0:1024], stg8[0:8, 0:128], ds["s_fl"],
                       waits=[cs])
            return cf

        c_gx = gx_dance(q_b, macc_c)
        c_q = P.op("vector", lambda v: v.reciprocal(rows0[:], rows2[:]), waits=[c_gx])
        c_q = P.op("vector", lambda v: v.tensor_scalar_mul(rows0[:], rows0[:], 128.0),
                   waits=[c_q])
        c_s1 = P.op("vector", lambda v: v.tensor_tensor(rows2[:], rows2[:], rows1[:],
                                                        ALU.mult), waits=[c_gx, r_c])
        c_s1 = P.op("vector", lambda v: v.tensor_scalar(rows2[:], rows2[:],
                                                        gmrow[0:1, 0:1], 1.0 / 128.0,
                                                        ALU.mult, ALU.mult),
                    waits=[c_s1, c_s])
        cqb = bcast_row(rows0, q_b, waits=[c_q, macc_c], pname="pB")
        csb = bcast_row(rows2, s_b, waits=[c_s1], pname="pB")
        xq_done = []
        for c in range(16):
            P.op("vector", (lambda v, c=c: v.tensor_tensor(t1a[:], xslice(xn, c), m_b[:],
                                                           ALU.subtract)),
                 waits=[cqb, csb] if c == 0 else [])
            P.op("vector", lambda v: v.tensor_tensor(t1b[:], t1a[:], q_b[:], ALU.mult))
            d = P.op("vector", (lambda v, c=c: v.tensor_scalar(xslice(qfb, c), t1b[:],
                                                               CLIP_HI, CLIP_LO,
                                                               ALU.min, ALU.max)))
            xq_done.append(d)
        c_xq = xq_done[-1]

        FS = ds["s_f0"], ds["s_f1"]
        fslab_free = [None, None]
        GO = ds["s_go0"], ds["s_go1"]
        gh_free = [None, None]
        gh_out = []
        for og in range(64):
            slot = og % 2
            loaded = None
            for c in range(16):
                loaded = P.dma("sync", fsb[slot][0:128, c * 128:(c + 1) * 128],
                               wfcT[c * 128:(c + 1) * 128, og * 128:(og + 1) * 128],
                               FS[slot],
                               waits=[fslab_free[slot], c_xq if og < 2 else None]
                               if c == 0 else [])
            tern_c = None
            for hf in range(2):
                cb = P.op("vector",
                          (lambda v, slot=slot, hf=hf: v.tensor_scalar(
                              t1a[0:128, 0:1024],
                              fsb[slot][0:128, hf * 1024:(hf + 1) * 1024],
                              thcol[0:128, 1:2], None, ALU.is_lt)),
                          waits=[loaded, c_th] if hf == 0 else [])
                tern_c = P.op("vector",
                              (lambda v, slot=slot, hf=hf: v.scalar_tensor_tensor(
                                  tsb[slot][0:128, hf * 1024:(hf + 1) * 1024],
                                  fsb[slot][0:128, hf * 1024:(hf + 1) * 1024],
                                  thcol[0:128, 0:1], t1a[0:128, 0:1024],
                                  ALU.is_gt, ALU.subtract)),
                              waits=[cb])
            fslab_free[slot] = tern_c
            pname = "pA" if (og % 2 == 0) else "pB"
            pp, pbk = PB[pname]
            for hf in range(2):
                pairs = [(tsb[slot][0:128, c * 128:(c + 1) * 128],
                          qfb[0:128, c * 1024 + hf * 512: c * 1024 + hf * 512 + 512])
                         for c in range(16)]
                mmc = mm_group(pname, hf, pairs, waits_first=[tern_c, c_xq])
            fslab_free[slot] = mmc
            ghslot = og % 2
            ev = psum_read("vector", pname,
                           (lambda v, pp=pp: v.tensor_tensor(t1b[:], pp[0:128, 0:1024],
                                                             s_b[:], ALU.mult)),
                           waits=[csb])
            gst = [hqx0, hqx1][ghslot]
            gg = P.op("scalar", (lambda s, gst=gst: s.activation(gst[:], t1b[:], AF.Gelu)),
                      waits=[ev, gh_free[ghslot]])
            for hf in range(2):
                mm_row_group("pD", hf, [gst[0:128, hf * 512:(hf + 1) * 512]],
                             waits_first=[gg], start=(og == 0), stop=(og == 63))
            sq = P.op("scalar", (lambda s, gst=gst: s.activation(sqo[:], gst[:], AF.Square)),
                      waits=[gg, sqo_free])
            for hf in range(2):
                c_sq = mm_row_group("pC", hf, [sqo[0:128, hf * 512:(hf + 1) * 512]],
                                    waits_first=[sq], start=(og == 0), stop=(og == 63))
            sqo_free = c_sq
            od = P.dma("gpsimd", gh_pg[og * 128:(og + 1) * 128, 0:TC], gst[:],
                       GO[ghslot], waits=[gg])
            gh_free[ghslot] = od
            gh_out.append(od)

        inv = 1.0 / float(FF)
        m_c = psum_read("vector", "pD",
                        (lambda v: v.tensor_scalar_mul(rows0[:], pD[0:1, 0:1024], inv)))
        e_c = psum_read("vector", "pC",
                        (lambda v: v.tensor_scalar_mul(rows1[:], pC[0:1, 0:1024], inv)))
        P.op("vector", lambda v: v.tensor_tensor(rows2[:], rows0[:], rows0[:], ALU.mult),
             waits=[m_c])
        P.op("vector", lambda v: v.tensor_tensor(rows1[:], rows1[:], rows2[:], ALU.subtract),
             waits=[e_c])
        vcomp = esems["vector"].at()
        sd = P.op("scalar", lambda s: s.activation(rows1[:], rows1[:], AF.Sqrt,
                                                   bias=eps6c[0:1, 0:1], scale=1.0),
                  waits=[vcomp])
        rh_c = P.op("vector", lambda v: v.reciprocal(rows1[:], rows1[:]), waits=[sd])
        cmb = bcast_row(rows0, m_b, waits=[m_c], pname="pB")

        GI = ds["s_gi0"], ds["s_gi1"]
        gi_free = [None, None]
        macc_c = None
        c_gh_all = ds["s_go0"].at(), ds["s_go1"].at()
        for fg in range(64):
            slot = fg % 2
            gst = [hqx0, hqx1][slot]
            gl = P.dma("gpsimd", gst[:], gh_pg[fg * 128:(fg + 1) * 128, 0:TC], GI[slot],
                       waits=([gi_free[slot]] + [c_gh_all[0], c_gh_all[1], cmb])
                       if fg < 2 else [gi_free[slot]])
            u = P.op("vector", (lambda v, gst=gst: v.tensor_tensor(t1a[:], gst[:], m_b[:],
                                                                   ALU.subtract)),
                     waits=[gl])
            gi_free[slot] = u
            a = P.op("scalar", lambda s: s.activation(sqo[:], t1a[:], AF.Abs),
                     waits=[u, sqo_free])
            if fg == 0:
                macc_c = P.op("vector", lambda v: v.tensor_copy(t1b[:], sqo[:]), waits=[a])
            else:
                macc_c = P.op("vector", lambda v: v.tensor_tensor(t1b[:], t1b[:], sqo[:],
                                                                  ALU.max), waits=[a])
            sqo_free = macc_c
        c_gxh = gx_dance(t1b, macc_c)
        c_q = P.op("vector", lambda v: v.reciprocal(rows0[:], rows2[:]), waits=[c_gxh])
        c_q = P.op("vector", lambda v: v.tensor_scalar_mul(rows0[:], rows0[:], 128.0),
                   waits=[c_q])
        c_s2 = P.op("vector", lambda v: v.tensor_tensor(rows2[:], rows2[:], rows1[:],
                                                        ALU.mult), waits=[c_gxh, rh_c])
        c_s2 = P.op("vector", lambda v: v.tensor_scalar(rows2[:], rows2[:],
                                                        gmrow[0:1, 1:2], 1.0 / 128.0,
                                                        ALU.mult, ALU.mult), waits=[c_s2])
        cqb = bcast_row(rows0, q_b, waits=[c_q], pname="pB")
        csb = bcast_row(rows2, s_b, waits=[c_s2], pname="pB")

        hq_done = []
        hq_st_prev = None
        giB_free = [None, macc_c]
        for fg in range(64):
            slot = fg % 2
            gst = [sqe, sqo][slot]
            gl = P.dma("gpsimd", gst[:], gh_pg[fg * 128:(fg + 1) * 128, 0:TC], GI[slot],
                       waits=[giB_free[slot],
                              cqb if fg < 2 else None, csb if fg < 2 else None,
                              gi_free[0] if fg == 0 else None,
                              gi_free[1] if fg == 1 else None])
            u = P.op("vector", (lambda v, gst=gst: v.tensor_tensor(t1a[:], gst[:], m_b[:],
                                                                   ALU.subtract)),
                     waits=[gl, c_pg_all])
            giB_free[slot] = u
            P.op("vector", lambda v: v.tensor_tensor(t1b[:], t1a[:], q_b[:], ALU.mult))
            if fg < 61:
                d = P.op("vector", (lambda v, fg=fg: v.tensor_scalar(hqt[fg][:], t1b[:],
                                                                     CLIP_HI, CLIP_LO,
                                                                     ALU.min, ALU.max)))
            else:
                oth = [sqo, sqe][fg % 2]
                d = P.op("vector", (lambda v, oth=oth: v.tensor_scalar(
                    oth[:], t1b[:], CLIP_HI, CLIP_LO, ALU.min, ALU.max)),
                    waits=[giB_free[1 - fg % 2]])
                hq_st_prev = P.dma("gpsimd", hq_pg[(fg - 61) * 128:(fg - 60) * 128, 0:TC],
                                   oth[:], ds["s_nr"], waits=[d])
                giB_free[1 - fg % 2] = hq_st_prev
            hq_done.append(d)
        c_hq = hq_done[-1]
        c_hq_st = hq_st_prev

        pslab_free = [fslab_free[0], fslab_free[1]]
        hq_free3 = [giB_free[0], giB_free[1], c_hq]
        y_done = []
        for og in range(8):
            hst = []
            for t3 in range(3):
                dst = [sqe, sqo, hqs2][t3]
                hst.append(P.dma("gpsimd", dst[:],
                                 hq_pg[t3 * 128:(t3 + 1) * 128, 0:TC], ds["s_nr"],
                                 waits=[hq_free3[t3],
                                        c_hq_st if og == 0 else None]))
            for sg in range(8):
                slot = sg % 2
                loaded = None
                for j in range(8):
                    fr = sg * 8 + j
                    loaded = P.dma("sync", fsb[slot][0:128, j * 256:(j + 1) * 256],
                                   wprojT[fr * 128:(fr + 1) * 128, og * 256:(og + 1) * 256],
                                   FS[slot],
                                   waits=[pslab_free[slot], c_hq if (og == 0 and sg < 2) else None]
                                   if j == 0 else [])
                tern_c = None
                for hf in range(2):
                    cb = P.op("vector",
                              (lambda v, slot=slot, hf=hf: v.tensor_scalar(
                                  t1a[0:128, 0:1024],
                                  fsb[slot][0:128, hf * 1024:(hf + 1) * 1024],
                                  thcol[0:128, 3:4], None, ALU.is_lt)),
                              waits=[loaded] if hf == 0 else [])
                    tern_c = P.op("vector",
                                  (lambda v, slot=slot, hf=hf: v.scalar_tensor_tensor(
                                      tsb[slot][0:128, hf * 1024:(hf + 1) * 1024],
                                      fsb[slot][0:128, hf * 1024:(hf + 1) * 1024],
                                      thcol[0:128, 2:3], t1a[0:128, 0:1024],
                                      ALU.is_gt, ALU.subtract)),
                                  waits=[cb])
                mmc = None
                for j in range(2):
                    oc = og * 2 + j
                    pname = "pA" if (j == 0) else "pB"
                    for hf in range(2):
                        def _hq(fc):
                            if fc < 61:
                                return hqt[fc]
                            return [sqe, sqo, hqs2][fc - 61]
                        pairs = [(tsb[slot][0:128, k * 256 + j * 128: k * 256 + j * 128 + 128],
                                  _hq(sg * 8 + k)[0:128, hf * 512:hf * 512 + 512])
                                 for k in range(8)]
                        mmc = mm_group(pname, hf, pairs,
                                       waits_first=[tern_c, hq_done[min(sg * 8 + 7, 60)],
                                                    hst[2] if sg == 7 else None],
                                       start=(sg == 0), stop=(sg == 7))
                pslab_free[slot] = mmc
            hq_free3 = [mmc, mmc, mmc]
            for j in range(2):
                oc = og * 2 + j
                pname = "pA" if (j == 0) else "pB"
                pp, pbk = PB[pname]
                e = psum_read("vector", pname,
                              (lambda v, oc=oc, pp=pp: v.tensor_tensor(
                                  xslice(ao, oc), pp[0:128, 0:1024], s_b[:], ALU.mult)))
                y_done.append(e)

        m_c, r_c = ln_stats(lambda c: xslice(ao, c), eps5c[0:1, 0:1], dep=[y_done[-1]])
        cmb = bcast_row(rows0, m_b, waits=[m_c], pname="pB")
        crb = bcast_row(rows1, r_b, waits=[r_c], pname="pB")
        c_ri = None
        for c in range(16):
            c_ri = P.dma("gpsimd", xslice(xres, c), xres_pg[c * 128:(c + 1) * 128, 0:TC],
                         ds["s_pgi"], waits=[y_done[-1]] if c == 0 else [])
        c_ri_all = ds["s_pgi"].at()
        out_c = None
        st_prev = [None, None]
        for c in range(16):
            ot = [t1a, t1b][c % 2]
            P.op("vector", (lambda v, c=c: v.tensor_tensor(s_b[:], xslice(ao, c), m_b[:],
                                                           ALU.subtract)),
                 waits=([cmb, crb, c_ri_all] if c == 0 else []))
            P.op("vector", lambda v: v.tensor_tensor(q_b[:], s_b[:], r_b[:], ALU.mult))
            d = P.op("vector", (lambda v, c=c, ot=ot: v.tensor_tensor(
                ot[:], q_b[:], xslice(xres, c), ALU.add)),
                waits=[st_prev[c % 2]])
            osem = ds["s_out"] if c % 2 == 0 else ds["s_o1"]
            w = P.dma("gpsimd", y_fm[c * 128:(c + 1) * 128, 0:TC], ot[:], osem,
                      waits=[d])
            st_prev[c % 2] = w
        P.raw("gpsimd", lambda g: (g.wait_ge(ds["s_out"].h, ds["s_out"].n),
                                   g.wait_ge(ds["s_o1"].h, ds["s_o1"].n)))
        P.raw("sync", lambda sy: (sy.wait_ge(ds["s_out"].h, ds["s_out"].n),
                                  sy.wait_ge(ds["s_o1"].h, ds["s_o1"].n)))

        P.emit(block)
    return nc


_NC_CACHE = None


def kernel(L0_attn_hidden_states, L0_attn_w_qkv, L0_attn_b_qkv, L0_attn_w_out,
           L0_attn_b_out, L0_ffn_w_fc, L0_ffn_w_proj):
    global _NC_CACHE
    x = np.asarray(L0_attn_hidden_states)
    wqkvT = np.ascontiguousarray(np.asarray(L0_attn_w_qkv).T)
    woutT = np.ascontiguousarray(np.asarray(L0_attn_w_out).T)
    wfcT = np.ascontiguousarray(np.asarray(L0_ffn_w_fc).T)
    wprojT = np.ascontiguousarray(np.asarray(L0_ffn_w_proj).T)
    bqkv_t = np.ascontiguousarray(np.asarray(L0_attn_b_qkv).reshape(48, 128).T)
    bout_t = np.ascontiguousarray(np.asarray(L0_attn_b_out).reshape(16, 128).T)

    if _NC_CACHE is None:
        _NC_CACHE = build_nc()
    nc = _NC_CACHE

    in_maps = []
    for r in range(NC):
        x_fm = np.ascontiguousarray(x[0, r * TC:(r + 1) * TC, :].T)
        gfc_sl = np.ascontiguousarray(wfcT[r * 256:(r + 1) * 256, :])
        gpj_sl = np.ascontiguousarray(wprojT[r * 1024:(r + 1) * 1024, :])
        in_maps.append({"x_fm": x_fm, "wqkvT": wqkvT, "woutT": woutT,
                        "wfcT": wfcT, "wprojT": wprojT, "bqkv_t": bqkv_t,
                        "bout_t": bout_t, "gfc_sl": gfc_sl, "gpj_sl": gpj_sl})
    res = run_bass_kernel_spmd(nc, in_maps, core_ids=list(range(NC)))
    out = np.empty((1, S, H), dtype=np.float32)
    for r in range(NC):
        out[0, r * TC:(r + 1) * TC, :] = res.results[r]["y_fm"].T
    return out


if __name__ == "__main__":
    build_nc()
    print("build ok")
